# version 05
# baseline (speedup 1.0000x reference)
import numpy as np
from contextlib import ExitStack
import concourse.bass as bass
import concourse.mybir as mybir
from concourse.bass_utils import run_bass_kernel_spmd

F32 = mybir.dt.float32
BF16 = mybir.dt.bfloat16
I32 = mybir.dt.int32
U8 = mybir.dt.uint8
AF = mybir.ActivationFunctionType
ALU = mybir.AluOpType

B, S, HID = 4, 4096, 2048
H, Dh, NH = 16, 128, 8
HD2 = NH * Dh
TILE = 512
NT = S // TILE
CH = 128
NCH = TILE // CH
THETA = 11.0
SCALE = float(Dh) ** -0.5
LNSCALE = float(np.log(SCALE))
EPS_L2, EPS_RMS, EPS_LN = 1e-6, 1e-5, 1e-5
TOK2 = S // 2
FT = 512
NFT = TOK2 // FT
IFF = 8192
CV = TILE + 3
CFG = dict(nt=NT, nhalf=2, nheads=NH, p2=True, nft=NFT, rs=99, nchunk=NCH)

_SMALL_SIZES = [("wfa", 2048), ("wga", 2048), ("wfb", 2048), ("wgb", 2048),
                ("wbp", 256), ("dtb", 16), ("convq", 64), ("convk", 64),
                ("convv", 64), ("alog", 32), ("eye", 128), ("ones", 128),
                ("negones", 128), ("oh16", 256), ("rmst", 128),
                ("sum2048", 1), ("sum1", 1), ("fconst", 8)]
SIDX = {}
_off = 0
for _n, _s in _SMALL_SIZES:
    SIDX[_n] = (_off, _s)
    _off += _s
WSMALL_F = _off

MASK_ORDER = ["p1", "l16", "l8", "leafA", "leafB"]


def _chunk_lhsT(W):
    K, M = W.shape
    mch, kch = M // 128, K // 128
    w4 = W.reshape(kch, 128, mch, 128)
    return np.ascontiguousarray(w4.transpose(2, 1, 0, 3)).reshape(mch, 128, kch * 128)


def _build_masks():
    j = np.arange(CH)[:, None]
    t = np.arange(CH)[None, :]
    blk16 = (j // 16 == t // 16)
    blk8 = (j // 8 == t // 8)
    lower = t > j
    l64 = (j < 64) & (t >= 64)
    l32 = ((j < 32) & (t >= 32) & (t < 64)) | ((j >= 64) & (j < 96) & (t >= 96))
    l16 = (j // 32 == t // 32) & (j // 16 != t // 16) & lower
    l8 = blk16 & (~blk8) & lower
    m = dict(p1=(l64 | l32), l16=l16, l8=l8, leafA=blk8 & lower,
             leafB=blk8 & (t >= j))
    return {k: np.tile(v.astype(np.uint8), (1, NCH)) for k, v in m.items()}


def _host_prep(inputs):
    x = np.asarray(inputs["L0_attn_hidden_states"], np.float32)
    x_fm = np.ascontiguousarray(x.transpose(0, 2, 1))
    Wq = np.asarray(inputs["L0_attn_Wq"], np.float32)
    Wk = np.asarray(inputs["L0_attn_Wk"], np.float32)
    Wv = np.asarray(inputs["L0_attn_Wv"], np.float32)
    Wb = np.asarray(inputs["L0_attn_Wb"], np.float32)
    Wfa = np.asarray(inputs["L0_attn_Wfa"], np.float32)
    Wfb = np.asarray(inputs["L0_attn_Wfb"], np.float32)
    Wga = np.asarray(inputs["L0_attn_Wga"], np.float32)
    Wgb = np.asarray(inputs["L0_attn_Wgb"], np.float32)
    Wo = np.asarray(inputs["L0_attn_Wo"], np.float32)
    A_log = np.asarray(inputs["L0_attn_A_log"], np.float32).reshape(H)
    dt_bias = np.asarray(inputs["L0_attn_dt_bias"], np.float32)
    rms_scale = np.asarray(inputs["L0_attn_rms_scale"], np.float32)
    conv = {t: np.asarray(inputs[f"L0_attn_conv_{t}"], np.float32)[:, 0, :]
            for t in "qkv"}
    masks = _build_masks()
    mask_pack = np.concatenate([masks[k] for k in MASK_ORDER], axis=1)
    oh16 = np.zeros((128, 256), np.float32)
    for g in range(16):
        oh16[:, g * 16 + g] = 1.0

    wqkv = np.concatenate([_chunk_lhsT(Wq), _chunk_lhsT(Wk),
                           _chunk_lhsT(Wv)], axis=0)
    alog_pad = np.zeros(32, np.float32)
    alog_pad[:16] = A_log
    smalls = {
        "wfa": _chunk_lhsT(Wfa).reshape(128, -1),
        "wga": _chunk_lhsT(Wga).reshape(128, -1),
        "wfb": Wfb,
        "wgb": Wgb,
        "wbp": np.ascontiguousarray(
            Wb.reshape(16, 128, 16).transpose(1, 0, 2)).reshape(128, 256),
        "dtb": dt_bias.reshape(16, 128).T,
        "convq": conv["q"].reshape(4, 16, 128).transpose(2, 1, 0).reshape(128, 64),
        "convk": conv["k"].reshape(4, 16, 128).transpose(2, 1, 0).reshape(128, 64),
        "convv": conv["v"].reshape(4, 16, 128).transpose(2, 1, 0).reshape(128, 64),
        "alog": np.tile(alog_pad[None, :], (128, 1)),
        "eye": np.eye(128, dtype=np.float32),
        "ones": np.ones((128, 128), np.float32),
        "negones": np.full((128, 128), -1.0, np.float32),
        "oh16": oh16,
        "rmst": np.tile(rms_scale[None, :], (128, 1)),
        "sum2048": np.full((128, 1), 1.0 / HID, np.float32),
        "sum1": np.ones((128, 1), np.float32),
        "fconst": np.tile(np.array([EPS_LN, EPS_L2, EPS_RMS, 1.0, LNSCALE,
                                    0.0, 0.0, 0.0], np.float32)[None, :],
                          (128, 1)),
    }
    wsmall = np.concatenate([smalls[n] for n, _ in _SMALL_SIZES],
                            axis=1).astype(np.float32)
    wo_pack = _chunk_lhsT(Wo)
    wg_pack = _chunk_lhsT(np.ascontiguousarray(
        np.asarray(inputs["L0_ffn_gate_w"], np.int32).T))
    wu_pack = _chunk_lhsT(np.ascontiguousarray(
        np.asarray(inputs["L0_ffn_up_w"], np.int32).T))
    wd_pack = _chunk_lhsT(np.ascontiguousarray(
        np.asarray(inputs["L0_ffn_down_w"], np.int32).T))
    ffnsc = np.concatenate([
        np.asarray(inputs["L0_ffn_gate_scale"], np.float32).reshape(64, 128).T,
        np.asarray(inputs["L0_ffn_up_scale"], np.float32).reshape(64, 128).T,
        np.asarray(inputs["L0_ffn_down_scale"], np.float32).reshape(16, 128).T],
        axis=1)
    return x_fm, wqkv, wsmall, wo_pack, wg_pack, wu_pack, wd_pack, ffnsc, mask_pack


class Prog:
    ENGINES = ["sync", "gpsimd", "tensor", "vector", "scalar"]

    def __init__(self, nc):
        self.nc = nc
        self.items = []
        self.sems = {}

    def emit(self, engine, fn, waits=(), tok=None, dma=False):
        inc = None
        if tok is not None:
            k = 16 if dma else 1
            self.sems[tok] = self.sems.get(tok, 0) + k
            inc = (tok, k)
        self.items.append((engine, fn, tuple(waits), inc))
        return self.sems.get(tok, 0) if tok else None

    def val(self, tok):
        return self.sems.get(tok, 0)

    def build(self, ctx):
        nc = self.nc
        hs = {name: ctx.enter_context(nc.semaphore(name)) for name in self.sems}
        block = ctx.enter_context(nc.Block())
        per_eng = {e: [] for e in self.ENGINES}
        for it in self.items:
            per_eng[it[0]].append(it)
        for ename in self.ENGINES:
            items = per_eng[ename]
            if not items:
                continue

            def make_body(items):
                def body(eng):
                    for (_, fn, waits, inc) in items:
                        for (sname, v) in waits:
                            eng.wait_ge(hs[sname], v)
                        instr = fn(eng)
                        if inc is not None:
                            instr.then_inc(hs[inc[0]], inc[1])
                return body
            getattr(block, ename)(make_body(items))
        return hs


def build_program(dbg_specs=()):
    nc = bass.Bass("TRN2", target_bir_lowering=False)
    d = {}
    d["x"] = nc.dram_tensor("x", [HID, S], F32, kind="ExternalInput").ap()
    d["x2"] = nc.dram_tensor("x2", [HID, TOK2], F32, kind="ExternalInput").ap()
    d["wqkv"] = nc.dram_tensor("wqkv", [48, 128, HID], F32, kind="ExternalInput").ap()
    d["wsmall"] = nc.dram_tensor("wsmall", [128, WSMALL_F], F32, kind="ExternalInput").ap()
    d["maskp"] = nc.dram_tensor("maskp", [128, 5 * TILE], U8, kind="ExternalInput").ap()
    d["wo"] = nc.dram_tensor("wo", [16, 128, HID], F32, kind="ExternalInput").ap()
    d["wg"] = nc.dram_tensor("wg", [64, 128, HID], I32, kind="ExternalInput").ap()
    d["wu"] = nc.dram_tensor("wu", [64, 128, HID], I32, kind="ExternalInput").ap()
    d["wd"] = nc.dram_tensor("wd", [16, 128, IFF], I32, kind="ExternalInput").ap()
    d["ffnsc"] = nc.dram_tensor("ffnsc", [128, 144], F32, kind="ExternalInput").ap()
    d["out"] = nc.dram_tensor("out", [HID, TOK2], F32, kind="ExternalOutput").ap()
    d["mixs_d"] = nc.dram_tensor("mixs_d", [HID, TOK2], BF16, kind="Internal").ap()
    d["msel"] = nc.dram_tensor("msel", [128, NT], F32, kind="ExternalInput").ap()
    d["rowf"] = nc.dram_tensor("rowf", [4096], F32, kind="Internal").ap()
    d["rowl"] = nc.dram_tensor("rowl", [NT + NFT, 16 * TILE], BF16, kind="Internal").ap()
    d["wg_bf"] = nc.dram_tensor("wg_bf", [64, 128, HID], BF16, kind="Internal").ap()
    d["wu_bf"] = nc.dram_tensor("wu_bf", [64, 128, HID], BF16, kind="Internal").ap()
    d["wd_bf"] = nc.dram_tensor("wd_bf", [16, 128, IFF], BF16, kind="Internal").ap()
    for (name, shp, dt) in dbg_specs:
        d[name] = nc.dram_tensor(name, shp, dt, kind="ExternalOutput").ap()

    prog = Prog(nc)
    with ExitStack() as ctx:
        _body(nc, prog, ctx, d, dbg={nm for (nm, _, _) in dbg_specs})
        prog.build(ctx)
    return nc


def _body(nc, prog, ctx, d, dbg):
    em, val = prog.emit, prog.val
    E = ctx.enter_context

    small = E(nc.sbuf_tensor("small", [128, WSMALL_F], BF16))
    smallf = E(nc.sbuf_tensor("smallf", [128, 256], F32))
    maskt = E(nc.sbuf_tensor("maskt", [128, 5 * TILE], U8))
    ffsc = E(nc.sbuf_tensor("ffsc", [128, 144], F32))
    msel = E(nc.sbuf_tensor("msel_sb", [128, NT], F32))
    abc = E(nc.sbuf_tensor("abc", [128, 16], F32))
    zerot = E(nc.sbuf_tensor("zerot", [128, 128], F32))
    rowraw = E(nc.sbuf_tensor("rowraw", [1, 8 * TILE], F32))
    rowst = rowraw
    rowbf = E(nc.sbuf_tensor("rowbf", [1, 2 * TILE], BF16))
    sid = {k: small[:, v[0]:v[0] + v[1]] for k, v in SIDX.items()}
    msk = {k: maskt[:, i * TILE:(i + 1) * TILE] for i, k in enumerate(MASK_ORDER)}
    dtb_f = smallf[:, 0:16]
    fc = lambda i: smallf[:, 32 + i:33 + i]
    fc1 = lambda i, p=1: smallf[0:p, 32 + i:33 + i]

    pproj = [E(nc.psum_tensor(f"pproj{i}", [128, TILE], F32)) for i in range(2)]
    pbc = E(nc.psum_tensor("pbc", [128, TILE], F32))
    pstat = E(nc.psum_tensor("pstat", [16, TILE], F32))
    ppc = [E(nc.psum_tensor(f"ppc{i}", [128, TILE], F32)) for i in range(2)]
    ptr = E(nc.psum_tensor("ptr", [128, TILE], BF16))
    pmisc = E(nc.psum_tensor("pmisc", [128, 256], F32))
    pseq = pmisc[:, 0:128]
    pg = pmisc[:, 128:256]

    em("gpsimd", lambda g: g.dma_start(out=small[:], in_=d["wsmall"][:]), tok="dq", dma=True)
    em("sync", lambda s: s.dma_start(out=maskt[:], in_=d["maskp"][:]), tok="ds", dma=True)
    em("sync", lambda s: s.dma_start(out=ffsc[:], in_=d["ffnsc"][:]), tok="ds", dma=True)
    em("sync", lambda s: s.dma_start(out=msel[:], in_=d["msel"][:]), tok="ds", dma=True)
    em("sync", lambda s: s.dma_start(
        out=smallf[:, 0:16],
        in_=d["wsmall"][:, SIDX["dtb"][0]:SIDX["dtb"][0] + 16]), tok="ds", dma=True)
    em("sync", lambda s: s.dma_start(
        out=smallf[:, 32:40],
        in_=d["wsmall"][:, SIDX["fconst"][0]:SIDX["fconst"][0] + 8]), tok="ds", dma=True)
    em("sync", lambda s: s.dma_start(
        out=smallf[:, 64:256],
        in_=d["wsmall"][:, SIDX["convq"][0]:SIDX["convq"][0] + 192]), tok="ds", dma=True)
    w_init, s_init = val("dq"), val("ds")

    em("vector", lambda v: v.memset(zerot[:], 0.0), tok="dve")
    alog_bf = small[0:1, SIDX["alog"][0]:SIDX["alog"][0] + 16]
    em("scalar", lambda s: s.activation(out=smallf[0:1, 16:32], in_=alog_bf,
                                        func=AF.Exp), waits=[("dq", w_init)], tok="act")
    em("vector", lambda v: v.tensor_copy(
        out=small[0:1, SIDX["alog"][0] + 16:SIDX["alog"][0] + 32],
        in_=smallf[0:1, 16:32]), waits=[("act", val("act"))], tok="dve")
    ealog_bf = small[0:1, SIDX["alog"][0] + 16:SIDX["alog"][0] + 32]
    for h in range(16):
        em("tensor", lambda t, h=h: t.matmul(pbc[:, h:h + 1],
                                             sid["negones"][0:1, :],
                                             ealog_bf[:, h:h + 1],
                                             start=(h == 0), stop=(h == 15)),
           waits=([("dve", val("dve")), ("dq", w_init)] if h == 0 else ()),
           tok=("pe" if h == 15 else None))
    em("vector", lambda v: v.tensor_copy(out=abc[:], in_=pbc[:, 0:16]),
       waits=[("pe", val("pe"))], tok="dve")
    init_done = val("dve")

    mix_ctx = ExitStack()
    Em = mix_ctx.enter_context
    sbm = lambda n, shp, dt=BF16: Em(nc.sbuf_tensor(n, shp, dt))
    xt = sbm("xt", [128, 16 * TILE])
    wbuf = [sbm(f"wbuf{i}", [128, 16 * 128]) for i in range(2)]
    x2t = sbm("x2t", [128, 16 * TILE])
    rbc = sbm("rbc", [128, TILE])
    csrow = sbm("csrow", [1, 52 * 128])
    qbar = sbm("qbar", [128, NH * TILE])
    kbar = sbm("kbar", [128, NH * TILE])
    vbar = sbm("vbar", [128, NH * TILE])
    qcv = sbm("qcv", [128, NH * CV])
    kcv = sbm("kcv", [128, NH * CV])
    vcv = sbm("vcv", [128, NH * CV])
    tails = sbm("tails", [128, 2 * 3 * NH * 3])
    fga = sbm("fga", [128, 2 * TILE])
    gdec = sbm("gdec", [128, TILE], F32)
    beta_b = sbm("beta_b", [16, TILE])
    bpre = sbm("bpre", [16, TILE], F32)
    betaT = sbm("betaT", [128, NCH * 16], F32)
    l2pk = sbm("l2pk", [16, TILE], F32)
    l2st = sbm("l2st", [16, TILE])
    l2row = rowraw[:].bitcast(BF16)
    Gt = sbm("Gt", [128, TILE], F32)
    fpk = sbm("fpk", [128, 2 * TILE], F32)
    epk = sbm("epk", [128, 2 * TILE])
    egc = sbm("egc", [128, NCH], F32)
    kfac = [sbm(f"kfac{l}", [128, TILE]) for l in range(4)]
    qfac = [sbm(f"qfac{l}", [128, TILE]) for l in range(4)]
    kfacL = sbm("kfacL", [128, TILE])
    qfacL = sbm("qfacL", [128, TILE])
    lcolL = sbm("lcolL", [128, TILE])
    kap = sbm("kap", [128, TILE])
    qhat = sbm("qhat", [128, TILE])
    phifm = sbm("phifm", [128, TILE])
    Dt = sbm("Dt", [128, TILE])
    Ft = sbm("Ft", [128, TILE])
    Bt = sbm("Bt", [128, TILE])
    Dn = sbm("Dn", [128, TILE])
    Dtp = {p: sbm(f"Dtp{p}", [128, TILE]) for p in (2, 4, 8)}
    Dnp = {p: sbm(f"Dnp{p}", [128, TILE]) for p in (2, 4, 8)}
    Mt = sbm("Mt", [128, TILE])
    Mx = sbm("Mx", [128, TILE])
    RHS = sbm("RHS", [128, 2 * TILE])
    Zc = sbm("Zc", [128, 2 * TILE])
    phitm = sbm("phitm", [128, TILE])
    Wt = sbm("Wt", [128, TILE])
    Pfm = sbm("Pfm", [128, TILE])
    Ovt = sbm("Ovt", [128, TILE])
    Svec = [sbm(f"S{h}", [128, 128], F32) for h in range(16)]
    Sbf = sbm("Sbf", [128, 128])
    Otile = sbm("Otile", [128, 128], F32)
    osq = sbm("osq", [128, 128], F32)
    rstd_c = sbm("rstd_c", [128, NCH], F32)
    gsig = sbm("gsig", [128, 128])
    mixt = sbm("mixt", [128, 128])
    mixf = sbm("mixf", [128, TILE])
    mixg = sbm("mixg", [128, TILE])

    def wsl(kc):
        return slice(kc * 128, (kc + 1) * 128)

    for tile in (qcv, kcv, vcv, Dt, Ft, Bt, Sbf, tails):
        em("vector", lambda v, tile=tile: v.memset(tile[:], 0.0), tok="dve")
    for h in range(16):
        em("vector", lambda v, h=h: v.memset(Svec[h][:], 0.0), tok="dve")
    init_clear = val("dve")

    res_lhs = {48: (lambda kc: sid["wbp"][:, kc * 16:(kc + 1) * 16]),
               49: (lambda kc: sid["wfa"][:, wsl(kc)]),
               50: (lambda kc: sid["wga"][:, wsl(kc)])}

    wfree = [("dve", init_clear), ("dve", init_clear)]
    for g in range(51):
        if g < 48:
            bi = g % 2
            wl = em("gpsimd", lambda gg, g=g, bi=bi: gg.dma_start(
                out=wbuf[bi][:], in_=d["wqkv"][g]), waits=[wfree[bi]], tok="dq", dma=True)
            lhs = lambda kc, bi=bi: wbuf[bi][:, wsl(kc)]
            wwait = [("dq", wl)]
            M = 128
        else:
            lhs = res_lhs[g]
            wwait = [("dq", w_init)]
            M = 16 if g == 48 else 128
        for kc in range(16):
            em("tensor", lambda t, kc=kc, lhs=lhs, g=g, M=M: t.matmul(
                pbc[0:M, g:g + 1], lhs(kc), sid["sum1"][:, 0:1],
                start=(kc == 0), stop=(kc == 15)),
               waits=(wwait + [("dve", init_clear), ("dve", init_done)]) if kc == 0 else (),
               tok=("pe" if kc == 15 else None))
        if g < 48:
            wfree[g % 2] = ("pe", val("pe"))
    cs_mm = val("pe")
    cs_col = x2t[:, 0:64]
    em("vector", lambda v: v.memset(cs_col[:, 51:64], 0.0), tok="dve")
    em("vector", lambda v: v.tensor_copy(out=cs_col[:, 0:51], in_=pbc[:, 0:51]),
       waits=[("pe", cs_mm)], tok="dve")
    em("tensor", lambda t: t.transpose(ptr[0:64, 0:128], cs_col[:, 0:64], sid["eye"]),
       waits=[("dve", val("dve"))], tok="pe")
    cs_tr = x2t[:, 64:192]
    em("vector", lambda v: v.tensor_copy(out=cs_tr[0:64, 0:128], in_=ptr[0:64, 0:128]),
       waits=[("pe", val("pe"))], tok="dve")
    em("sync", lambda s: s.dma_start(
        out=d["rowf"].rearrange("(p f) -> p f", p=64).bitcast(BF16)[:, 0:128],
        in_=cs_tr[0:64, 0:128]),
       waits=[("dve", val("dve"))], tok="ds", dma=True)
    em("sync", lambda s: s.dma_start(
        out=csrow[0:1, 0:52 * 128][0:1, 0:8192 if False else 52 * 128],
        in_=d["rowf"].rearrange("(o f) -> o f", o=1).bitcast(BF16)[0:1, 0:52 * 128]),
       waits=[("ds", val("ds"))], tok="ds", dma=True)
    csrow_done = val("ds")

    state = dict(wfree=[("pe", cs_mm), ("pe", cs_mm)])

    def dump(name, src_ap, cast=False):
        if name not in dbg:
            return
        eng = "gpsimd" if cast else "sync"
        tk = "dq" if cast else "ds"
        em(eng, lambda s, name=name, src_ap=src_ap: s.dma_start(
            out=d[name][:], in_=src_ap),
           waits=[("dve", val("dve")), ("act", val("act")), ("pe", val("pe"))],
           tok=tk, dma=True)

    LVL = [(128, 64), (64, 32), (32, 16), (16, 8)]

    def _recur_chunk(ti, hg, c):
        cs = slice(c * 128, (c + 1) * 128)
        c2 = slice(c * 256, (c + 1) * 256)
        bcol = betaT[:, c * 16 + hg:c * 16 + hg + 1]
        q0 = lambda i: ppc[0][:, i * 128:(i + 1) * 128]
        q1 = lambda i: ppc[1][:, i * 128:(i + 1) * 128]

        em("tensor", lambda t: t.matmul(ppc[0][0:64, 64:128],
                                        kfac[0][:, c * 128:c * 128 + 64],
                                        kfac[0][:, c * 128 + 64:c * 128 + 128],
                                        start=True, stop=True),
           waits=[("dve", val("dve"))], tok="pe")
        em("tensor", lambda t: t.matmul(ppc[1][0:64, 64:128],
                                        kfac[0][:, c * 128:c * 128 + 64],
                                        qfac[0][:, c * 128 + 64:c * 128 + 128],
                                        start=True, stop=True), tok="pe")
        for (j0, t0) in ((0, 32), (64, 96)):
            em("tensor", lambda t, j0=j0, t0=t0: t.matmul(
                ppc[0][j0:j0 + 32, t0:t0 + 32],
                kfac[1][:, c * 128 + j0:c * 128 + j0 + 32],
                kfac[1][:, c * 128 + t0:c * 128 + t0 + 32],
                start=True, stop=True), tok="pe")
            em("tensor", lambda t, j0=j0, t0=t0: t.matmul(
                ppc[1][j0:j0 + 32, t0:t0 + 32],
                kfac[1][:, c * 128 + j0:c * 128 + j0 + 32],
                qfac[1][:, c * 128 + t0:c * 128 + t0 + 32],
                start=True, stop=True), tok="pe")
        em("tensor", lambda t: t.matmul(q0(1), kfac[2][:, cs], kfac[2][:, cs],
                                        start=True, stop=True), tok="pe")
        em("tensor", lambda t: t.matmul(q1(1), kfac[2][:, cs], qfac[2][:, cs],
                                        start=True, stop=True), tok="pe")
        em("tensor", lambda t: t.matmul(q0(2), kfac[3][:, cs], kfac[3][:, cs],
                                        start=True, stop=True), tok="pe")
        em("tensor", lambda t: t.matmul(q1(2), kfac[3][:, cs], qfac[3][:, cs],
                                        start=True, stop=True), tok="pe")
        em("tensor", lambda t: t.matmul(q0(3), lcolL[:, cs], kfacL[:, cs],
                                        start=True, stop=True), tok="pe")
        em("tensor", lambda t: t.matmul(q1(3), lcolL[:, cs], qfacL[:, cs],
                                        start=True, stop=True), tok="pe")
        pieces = val("pe")
        if CFG["rs"] < 5:
            return
        mc = lambda k: msk[k][:, cs]
        em("vector", lambda v: v.copy_predicated(out=Ft[:, cs], mask=mc("p1"),
                                                 data=q0(0)),
           waits=[("pe", pieces)], tok="dve")
        em("vector", lambda v: v.copy_predicated(out=Ft[:, cs], mask=mc("l16"),
                                                 data=q0(1)), tok="dve")
        em("vector", lambda v: v.copy_predicated(out=Dt[:, cs], mask=mc("l8"),
                                                 data=q0(2)), tok="dve")
        em("vector", lambda v: v.copy_predicated(out=Dt[:, cs], mask=mc("leafA"),
                                                 data=q0(3)), tok="dve")
        em("vector", lambda v: v.copy_predicated(out=Bt[:, cs], mask=mc("p1"),
                                                 data=q1(0)), tok="dve")
        em("vector", lambda v: v.copy_predicated(out=Bt[:, cs], mask=mc("l16"),
                                                 data=q1(1)), tok="dve")
        em("vector", lambda v: v.copy_predicated(out=Bt[:, cs], mask=mc("l8"),
                                                 data=q1(2)), tok="dve")
        em("vector", lambda v: v.copy_predicated(out=Bt[:, cs], mask=mc("leafB"),
                                                 data=q1(3)), tok="dve")
        em("vector", lambda v: v.tensor_scalar(out=Ft[:, cs], in0=Ft[:, cs],
                                               scalar1=bcol, scalar2=-1.0,
                                               op0=ALU.mult, op1=ALU.mult), tok="dve")
        em("vector", lambda v: v.tensor_scalar(out=Dt[:, cs], in0=Dt[:, cs],
                                               scalar1=bcol, scalar2=-1.0,
                                               op0=ALU.mult, op1=ALU.mult), tok="dve")
        em("vector", lambda v: v.tensor_scalar(out=Bt[:, cs], in0=Bt[:, cs],
                                               scalar1=bcol, scalar2=SCALE,
                                               op0=ALU.mult, op1=ALU.mult), tok="dve")
        asm = val("dve")
        if CFG["rs"] < 6:
            return
        em("tensor", lambda t: t.transpose(ptr[:, 0:128], Dt[:, cs], sid["eye"]),
           waits=[("dve", asm)], tok="pe")
        em("vector", lambda v: v.tensor_copy(out=Dn[:, cs], in_=ptr[:, 0:128]),
           waits=[("pe", val("pe"))], tok="dve")
        em("tensor", lambda t: t.transpose(
            ptr[:, 128:256], vbar[:, (hg % NH) * TILE + c * 128:
                                  (hg % NH) * TILE + (c + 1) * 128],
            sid["eye"]), waits=[("dve", val("dve"))], tok="pe")
        em("vector", lambda v: v.tensor_copy(out=RHS[:, c * 256:c * 256 + 128],
                                             in_=ptr[:, 128:256]),
           waits=[("pe", val("pe"))], tok="dve")
        em("tensor", lambda t: t.transpose(ptr[:, 256:384], kap[:, cs], sid["eye"]),
           waits=[("dve", val("dve"))], tok="pe")
        em("vector", lambda v: v.tensor_copy(out=RHS[:, c * 256 + 128:c * 256 + 256],
                                             in_=ptr[:, 256:384]),
           waits=[("pe", val("pe"))], tok="dve")
        em("tensor", lambda t: t.transpose(ptr[:, 384:512], phifm[:, cs], sid["eye"]),
           waits=[("dve", val("dve"))], tok="pe")
        em("vector", lambda v: v.tensor_scalar(out=phitm[:, cs], in0=ptr[:, 384:512],
                                               scalar1=bcol, scalar2=None,
                                               op0=ALU.mult),
           waits=[("pe", val("pe"))], tok="dve")

        if CFG["rs"] < 7:
            return
        def sq_mm(lhsT, rhs, dst):
            em("tensor", lambda t, lhsT=lhsT, rhs=rhs: t.matmul(
                ppc[0][:, 0:128], lhsT, rhs, start=True, stop=True),
               waits=[("dve", val("dve"))], tok="pe")
            em("vector", lambda v, dst=dst: v.tensor_copy(out=dst,
                                                          in_=ppc[0][:, 0:128]),
               waits=[("pe", val("pe"))], tok="dve")
        sq_mm(Dn[:, cs], Dt[:, cs], Dtp[2][:, cs])
        sq_mm(Dt[:, cs], Dn[:, cs], Dnp[2][:, cs])
        sq_mm(Dnp[2][:, cs], Dtp[2][:, cs], Dtp[4][:, cs])
        sq_mm(Dtp[2][:, cs], Dnp[2][:, cs], Dnp[4][:, cs])
        sq_mm(Dnp[4][:, cs], Dtp[4][:, cs], Dtp[8][:, cs])
        sq_mm(Dtp[4][:, cs], Dnp[4][:, cs], Dnp[8][:, cs])

        if CFG["rs"] < 8:
            return
        msrc = Ft
        for i4, p in enumerate([None, 2, 4, 8]):
            lhsT = Dn[:, cs] if p is None else Dnp[p][:, cs]
            dst = Mx if (i4 % 2 == 0) else Mt
            em("tensor", lambda t, lhsT=lhsT, msrc=msrc: t.matmul(
                ppc[1][:, 0:128], lhsT, msrc[:, cs], start=True, stop=True),
               waits=[("dve", val("dve"))], tok="pe")
            em("vector", lambda v, msrc=msrc, dst=dst: v.tensor_add(
                out=dst[:, cs], in0=msrc[:, cs], in1=ppc[1][:, 0:128]),
               waits=[("pe", val("pe"))], tok="dve")
            msrc = dst
        Mfin = msrc

        if CFG["rs"] < 9:
            return
        zsrc = RHS
        for i5 in range(7):
            em("tensor", lambda t, zsrc=zsrc: t.matmul(
                ppc[0][:, 256:512], Mfin[:, cs], zsrc[:, c2],
                start=True, stop=True),
               waits=[("dve", val("dve"))], tok="pe")
            em("vector", lambda v: v.tensor_add(out=Zc[:, c2], in0=RHS[:, c2],
                                                in1=ppc[0][:, 256:512]),
               waits=[("pe", val("pe"))], tok="dve")
            zsrc = Zc

        if CFG["rs"] < 10:
            return
        for p in [None, 2, 4, 8]:
            lhsT = Dt[:, cs] if p is None else Dtp[p][:, cs]
            em("tensor", lambda t, lhsT=lhsT: t.matmul(
                ppc[1][:, 256:512], lhsT, Zc[:, c2], start=True, stop=True),
               waits=[("dve", val("dve"))], tok="pe")
            em("vector", lambda v: v.tensor_add(out=Zc[:, c2], in0=Zc[:, c2],
                                                in1=ppc[1][:, 256:512]),
               waits=[("pe", val("pe"))], tok="dve")
        solve_done = val("dve")
        EvS = Zc[:, c * 256:c * 256 + 128]
        TKS = Zc[:, c * 256 + 128:c * 256 + 256]

        if CFG["rs"] < 11:
            return
        em("tensor", lambda t: t.matmul(ppc[0][:, 0:128], TKS, phitm[:, cs],
                                        start=True, stop=True),
           waits=[("dve", solve_done)], tok="pe")
        em("scalar", lambda s: s.mul(out=Wt[:, cs], in_=ppc[0][:, 0:128], mul=-1.0),
           waits=[("pe", val("pe"))], tok="act")
        em("tensor", lambda t: t.matmul(ppc[1][:, 0:128], TKS, Bt[:, cs],
                                        start=True, stop=True), tok="pe")
        em("scalar", lambda s: s.mul(out=Pfm[:, cs], in_=ppc[1][:, 0:128], mul=-1.0),
           waits=[("pe", val("pe"))], tok="act")
        em("tensor", lambda t: t.matmul(ppc[0][:, 256:384], Bt[:, cs], EvS,
                                        start=True, stop=True),
           waits=[("act", val("act"))], tok="pe")
        em("vector", lambda v: v.tensor_copy(out=Ovt[:, cs], in_=ppc[0][:, 256:384]),
           waits=[("pe", val("pe"))], tok="dve")
        wpo_a, wpo_d = val("act"), val("dve")

        if CFG["rs"] < 12:
            return
        em("tensor", lambda t: t.matmul(pg, qhat[:, cs], Sbf[:],
                                        start=True, stop=False),
           waits=[("act", wpo_a), ("dve", wpo_d)], tok="pe")
        em("tensor", lambda t: t.matmul(pg, Pfm[:, cs], Sbf[:],
                                        start=False, stop=True), tok="pe")
        o_mm = val("pe")
        em("vector", lambda v: v.tensor_add(out=Otile[:], in0=Ovt[:, cs], in1=pg),
           waits=[("pe", o_mm)], tok="dve")
        o_done = val("dve")
        em("tensor", lambda t: t.matmul(pseq, Wt[:, cs], Sbf[:],
                                        start=True, stop=False),
           waits=[("dve", o_done)], tok="pe")
        em("tensor", lambda t: t.matmul(pseq, phitm[:, cs], EvS,
                                        start=False, stop=True), tok="pe")
        s_mm = val("pe")
        em("vector", lambda v, hg=hg, c=c: v.scalar_tensor_tensor(
            out=Svec[hg][:], in0=Svec[hg][:],
            scalar=egc[:, c:c + 1],
            in1=pseq, op0=ALU.mult, op1=ALU.add),
           waits=[("pe", s_mm)], tok="dve")
        em("vector", lambda v, hg=hg: v.tensor_copy(out=Sbf[:], in_=Svec[hg][:]),
           tok="dve")

        if CFG["rs"] < 13:
            return
        em("vector", lambda v, c=c: v.scalar_tensor_tensor(
            out=osq[:], in0=Otile[:], scalar=1.0, in1=Otile[:],
            op0=ALU.mult, op1=ALU.mult, accum_out=rstd_c[:, c:c + 1]),
           tok="dve")
        rsq_acc = val("dve")
        em("scalar", lambda s, c=c: s.activation(
            out=rstd_c[:, c:c + 1], in_=rstd_c[:, c:c + 1], func=AF.Ln,
            scale=1.0 / 128.0, bias=fc(2)),
           waits=[("dve", rsq_acc)], tok="act")
        em("scalar", lambda s, c=c: s.activation(
            out=rstd_c[:, c:c + 1], in_=rstd_c[:, c:c + 1], func=AF.Exp,
            scale=-0.5), tok="act")
        rstd_ok = val("act")
        em("tensor", lambda t, hg=hg, c=c: t.matmul(
            pg, fga[:, TILE + c * 128:TILE + (c + 1) * 128],
            sid["wgb"][:, hg * 128:(hg + 1) * 128], start=True, stop=True),
           waits=[("dve", val("dve"))], tok="pe")
        em("scalar", lambda s: s.activation(out=gsig[:], in_=pg, func=AF.Sigmoid),
           waits=[("pe", val("pe"))], tok="act")
        g_ok = val("act")
        em("vector", lambda v, c=c: v.tensor_scalar(
            out=osq[:], in0=Otile[:], scalar1=rstd_c[:, c:c + 1], scalar2=None,
            op0=ALU.mult), waits=[("act", rstd_ok)], tok="dve")
        em("vector", lambda v: v.tensor_mul(out=osq[:], in0=osq[:], in1=sid["rmst"]),
           tok="dve")
        em("vector", lambda v: v.tensor_mul(out=mixt[:], in0=osq[:], in1=gsig[:]),
           waits=[("act", g_ok)], tok="dve")
        em("tensor", lambda t: t.transpose(ptr[:, 0:128], mixt[:], sid["eye"]),
           waits=[("dve", val("dve"))], tok="pe")
        em("vector", lambda v, c=c: v.tensor_copy(out=mixf[:, cs], in_=ptr[:, 0:128]),
           waits=[("pe", val("pe")), ("dq", val("dq"))], tok="dve")

    def _recur_head(ti, hh2, hg):
        kb = kbar[:, (hg % NH) * TILE:(hg % NH + 1) * TILE]
        qb = qbar[:, (hg % NH) * TILE:(hg % NH + 1) * TILE]

        pp = pproj[hg % 2]
        em("tensor", lambda t, hg=hg, pp=pp: t.matmul(
            pp[:], sid["wfb"][:, hg * 128:(hg + 1) * 128], fga[:, 0:TILE],
            start=True, stop=True),
           waits=[("dve", val("dve")), ("act", val("act"))], tok="pe")
        gl_mm = val("pe")
        sc1 = x2t[:, 2 * TILE:3 * TILE]
        sc2 = x2t[:, 3 * TILE:4 * TILE]
        em("scalar", lambda s, pp=pp, hg=hg, sc1=sc1: s.activation(
            out=sc1, in_=pp[:], func=AF.Abs, bias=dtb_f[:, hg:hg + 1]),
           waits=[("pe", gl_mm)], tok="act")
        em("scalar", lambda s, sc1=sc1, sc2=sc2: s.activation(
            out=sc2, in_=sc1, func=AF.Exp, scale=-1.0), tok="act")
        em("scalar", lambda s, sc2=sc2: s.activation(
            out=sc2, in_=sc2, func=AF.Ln, bias=fc(3)), tok="act")
        sp_act = val("act")
        em("vector", lambda v, pp=pp, hg=hg: v.tensor_scalar(
            out=gdec[:], in0=pp[:], scalar1=dtb_f[:, hg:hg + 1], scalar2=0.0,
            op0=ALU.add, op1=ALU.max),
           waits=[("act", sp_act)], tok="dve")
        em("vector", lambda v, sc2=sc2: v.tensor_add(
            out=gdec[:], in0=gdec[:], in1=sc2), tok="dve")
        em("vector", lambda v, hg=hg: v.tensor_scalar(
            out=gdec[:], in0=gdec[:], scalar1=abc[:, hg:hg + 1], scalar2=-THETA,
            op0=ALU.mult, op1=ALU.max), tok="dve")

        if ti == 0 and hg == 0:
            dump("dbg_g", gdec[:])
        if CFG["rs"] < 1:
            return
        for c in range(NCH):
            em("vector", lambda v, c=c: v.tensor_tensor_scan(
                out=Gt[:, c * 128:(c + 1) * 128],
                data0=gdec[:, c * 128:(c + 1) * 128],
                data1=zerot[:], initial=0.0, op0=ALU.add, op1=ALU.add),
               waits=([("dve", val("dve"))] if c == 0 else ()), tok="dve")
        scan_done = val("dve")

        if ti == 0 and hg == 0:
            dump("dbg_Gt", Gt[:])
        if CFG["rs"] < 2:
            return
        for bb in range(2):
            for li2 in range(2):
                li = bb * 2 + li2
                sp, hs2 = LVL[li]
                nb = TILE // sp
                sec = fpk[:, li2 * TILE:(li2 + 1) * TILE].rearrange(
                    "p (n b) -> p n b", b=sp)
                Gv = Gt[:].rearrange("p (n b) -> p n b", b=sp)
                anc = Gv[:, :, hs2:hs2 + 1].broadcast_to([128, nb, hs2])
                em("vector", lambda v, sec=sec, Gv=Gv, anc=anc, hs2=hs2, sp=sp:
                   v.tensor_sub(out=sec[:, :, hs2:sp], in0=Gv[:, :, hs2:sp], in1=anc),
                   waits=([("dve", scan_done), ("act", val("act"))]
                          if li2 == 0 else ()), tok="dve")
                em("vector", lambda v, sec=sec, Gv=Gv, anc=anc, hs2=hs2:
                   v.tensor_sub(out=sec[:, :, 0:hs2], in0=anc, in1=Gv[:, :, 0:hs2]),
                   tok="dve")
            argA = val("dve")
            em("scalar", lambda s: s.activation(out=epk[:], in_=fpk[:], func=AF.Exp),
               waits=[("dve", argA)], tok="act")
            expA = val("act")
            for li2 in range(2):
                li = bb * 2 + li2
                em("vector", lambda v, li=li, li2=li2, kb=kb: v.tensor_mul(
                    out=kfac[li][:], in0=kb,
                    in1=epk[:, li2 * TILE:(li2 + 1) * TILE]),
                   waits=([("act", expA)] if li2 == 0 else ()), tok="dve")
                em("vector", lambda v, li=li, li2=li2, qb=qb: v.tensor_mul(
                    out=qfac[li][:], in0=qb,
                    in1=epk[:, li2 * TILE:(li2 + 1) * TILE]),
                   tok="dve")
        facA = val("dve")

        if CFG["rs"] < 3:
            return
        G8 = Gt[:].rearrange("p (n b) -> p n b", b=8)
        a8 = G8[:, :, 0:1].broadcast_to([128, TILE // 8, 8])
        em("vector", lambda v: v.tensor_sub(
            out=fpk[:, 0:TILE].rearrange("p (n b) -> p n b", b=8), in0=G8, in1=a8),
           waits=[("dve", facA)], tok="dve")
        em("vector", lambda v: v.tensor_sub(
            out=fpk[:, TILE:2 * TILE].rearrange("p (n b) -> p n b", b=8),
            in0=a8, in1=G8), tok="dve")

        argB = val("dve")
        em("scalar", lambda s: s.activation(out=epk[:, 0:2 * TILE],
                                            in_=fpk[:, 0:2 * TILE], func=AF.Exp),
           waits=[("dve", argB)], tok="act")
        em("scalar", lambda s: s.activation(
            out=egc[:], in_=Gt[:].rearrange("p (n b) -> p n b", b=128)[:, :, 127],
            func=AF.Exp), tok="act")
        expB = val("act")
        em("vector", lambda v, kb=kb: v.tensor_mul(out=kfacL[:], in0=kb,
                                                   in1=epk[:, 0:TILE]),
           waits=[("act", expB)], tok="dve")
        em("vector", lambda v, qb=qb: v.tensor_mul(out=qfacL[:], in0=qb,
                                                   in1=epk[:, 0:TILE]), tok="dve")
        em("vector", lambda v, kb=kb: v.tensor_mul(out=lcolL[:], in0=kb,
                                                   in1=epk[:, TILE:2 * TILE]), tok="dve")
        G128 = Gt[:].rearrange("p (n b) -> p n b", b=128)
        aC = G128[:, :, 127:128].broadcast_to([128, NCH, 128])
        em("vector", lambda v: v.tensor_sub(
            out=fpk[:, 0:TILE].rearrange("p (n b) -> p n b", b=128),
            in0=aC, in1=G128), tok="dve")
        phiarg = val("dve")
        em("scalar", lambda s: s.activation(out=epk[:, 0:TILE],
                                            in_=fpk[:, 0:TILE], func=AF.Exp),
           waits=[("dve", phiarg)], tok="act")
        em("vector", lambda v, kb=kb: v.tensor_mul(out=phifm[:], in0=kb,
                                                   in1=epk[:, 0:TILE]),
           waits=[("act", val("act"))], tok="dve")
        em("scalar", lambda s: s.activation(out=epk[:, TILE:2 * TILE], in_=Gt[:],
                                            func=AF.Exp),
           waits=[("dve", val("dve"))], tok="act")
        em("scalar", lambda s: s.activation(out=epk[:, 0:TILE], in_=Gt[:],
                                            func=AF.Exp, bias=fc(4)), tok="act")
        eg_act = val("act")
        em("vector", lambda v, kb=kb: v.tensor_mul(out=kap[:], in0=kb,
                                                   in1=epk[:, TILE:2 * TILE]),
           waits=[("act", eg_act)], tok="dve")
        em("vector", lambda v, qb=qb: v.tensor_mul(out=qhat[:], in0=qb,
                                                   in1=epk[:, 0:TILE]),
           tok="dve")
        em("vector", lambda v, hg=hg: v.tensor_copy(out=Sbf[:], in_=Svec[hg][:]),
           tok="dve")

        if ti == 0 and hg == 0:
            dump("dbg_kfac0", kfac[0][:])
            dump("dbg_kfac3", kfac[3][:])
            dump("dbg_kfacL", kfacL[:])
            dump("dbg_lcolL", lcolL[:])
            dump("dbg_kap", kap[:])
            dump("dbg_qhat", qhat[:])
            dump("dbg_phifm", phifm[:])
        if CFG["rs"] < 4:
            return
        for c in range(CFG["nchunk"]):
            _recur_chunk(ti, hg, c)
        if ti == 0 and hg == 0:
            dump("dbg_Ft", Ft[:])
            dump("dbg_Dt", Dt[:])
            dump("dbg_Bt", Bt[:])
            dump("dbg_Zc", Zc[:])
            dump("dbg_Wt", Wt[:])
            dump("dbg_Pfm", Pfm[:])
            dump("dbg_Ovt", Ovt[:])
            dump("dbg_phitm", phitm[:])
            dump("dbg_S0", Svec[0][:])
            dump("dbg_mixf", mixf[:])

        if CFG["rs"] < 14:
            return
        em("vector", lambda v, ti=ti: v.tensor_scalar(
            out=mixg[:], in0=mixf[:], scalar1=msel[:, ti:ti + 1],
            scalar2=None, op0=ALU.mult),
           waits=[("dve", val("dve"))], tok="dve")
        acc = mybir.AluOpType.add if ti >= 4 else mybir.AluOpType.bypass
        em("gpsimd", lambda g, ti=ti, hg=hg, acc=acc: g.dma_start(
            out=d["mixs_d"][hg * 128:(hg + 1) * 128,
                            (ti % 4) * TILE:(ti % 4 + 1) * TILE],
            in_=mixg[:], accum_op=acc),
           waits=[("dve", val("dve"))], tok="dq", dma=True)

    def emit_tile(ti):
        src = d["x"].rearrange("(kc kp) s -> kp kc s", kp=128)[
            :, :, ti * TILE:(ti + 1) * TILE]
        xload = em("gpsimd", lambda g: g.dma_start(
            out=xt[:].rearrange("p (kc t) -> p kc t", kc=16), in_=src),
            waits=[("pe", val("pe")), ("act", val("act")), ("dve", val("dve"))],
            tok="dq", dma=True)
        xv = lambda kc: xt[:, kc * TILE:(kc + 1) * TILE]

        for kc in range(16):
            em("tensor", lambda t, kc=kc: t.matmul(
                pstat[0:1, :], sid["sum2048"], xv(kc),
                start=(kc == 0), stop=(kc == 15)),
               waits=([("dq", xload), ("ds", csrow_done)] if kc == 0 else ()),
               tok=("pe" if kc == 15 else None))
        em("vector", lambda v: v.tensor_copy(out=rowst[0:1, 0:TILE], in_=pstat[0:1, :]),
           waits=[("pe", val("pe"))], tok="dve")
        m_ev = val("dve")
        em("scalar", lambda s: s.activation(out=x2t[:], in_=xt[:], func=AF.Square),
           waits=[("dq", xload)], tok="act")
        sq_done = val("act")
        for kc in range(16):
            em("tensor", lambda t, kc=kc: t.matmul(
                pstat[0:1, :], sid["sum2048"], x2t[:, kc * TILE:(kc + 1) * TILE],
                start=(kc == 0), stop=(kc == 15)),
               waits=([("act", sq_done), ("dve", m_ev)] if kc == 0 else ()),
               tok=("pe" if kc == 15 else None))
        em("vector", lambda v: v.tensor_copy(out=rowst[0:1, TILE:2 * TILE],
                                             in_=pstat[0:1, :]),
           waits=[("pe", val("pe"))], tok="dve")
        em("vector", lambda v: v.tensor_mul(out=rowst[0:1, 2 * TILE:3 * TILE],
                                            in0=rowst[0:1, 0:TILE],
                                            in1=rowst[0:1, 0:TILE]), tok="dve")
        em("vector", lambda v: v.tensor_sub(out=rowst[0:1, 2 * TILE:3 * TILE],
                                            in0=rowst[0:1, TILE:2 * TILE],
                                            in1=rowst[0:1, 2 * TILE:3 * TILE]),
           tok="dve")
        vdone = val("dve")
        em("scalar", lambda s: s.activation(out=rowst[0:1, 3 * TILE:4 * TILE],
                                            in_=rowst[0:1, 2 * TILE:3 * TILE],
                                            func=AF.Ln, bias=fc1(0)),
           waits=[("dve", vdone)], tok="act")
        em("scalar", lambda s: s.activation(out=rowst[0:1, 4 * TILE:5 * TILE],
                                            in_=rowst[0:1, 3 * TILE:4 * TILE],
                                            func=AF.Exp, scale=-0.5), tok="act")
        rstd_a = val("act")
        em("vector", lambda v: v.tensor_scalar(out=rowbf[0:1, 0:TILE],
                                               in0=rowst[0:1, 0:TILE],
                                               scalar1=-1.0, scalar2=None,
                                               op0=ALU.mult),
           waits=[("act", rstd_a)], tok="dve")
        em("vector", lambda v: v.tensor_copy(out=rowbf[0:1, TILE:2 * TILE],
                                             in_=rowst[0:1, 4 * TILE:5 * TILE]),
           tok="dve")
        row_done = val("dve")
        if ti == 0:
            dump("dbg_rows", rowst[0:1, 0:5 * TILE])
        mrow = rowbf[0:1, 0:TILE]
        em("tensor", lambda t: t.matmul(pbc[:], sid["ones"][0:1, :],
                                        rowbf[0:1, TILE:2 * TILE],
                                        start=True, stop=True),
           waits=[("dve", row_done)], tok="pe")
        em("vector", lambda v: v.tensor_copy(out=rbc[:], in_=pbc[:]),
           waits=[("pe", val("pe"))], tok="dve")
        rbc_done = val("dve")

        def proj_group(g, dest_fn):
            pp = pproj[g % 2]
            if g < 48:
                bi = g % 2
                wl = em("gpsimd", lambda gg, g=g, bi=bi: gg.dma_start(
                    out=wbuf[bi][:], in_=d["wqkv"][g]),
                    waits=[state["wfree"][bi]], tok="dq", dma=True)
                lhs = lambda kc, bi=bi: wbuf[bi][:, wsl(kc)]
                wwaits = [("dq", wl)]
                M = 128
            else:
                lhs = res_lhs[g]
                wwaits = [("dq", w_init)]
                M = 16 if g == 48 else 128
            for kc in range(16):
                em("tensor", lambda t, kc=kc, lhs=lhs, pp=pp, M=M: t.matmul(
                    pp[0:M, :], lhs(kc), xv(kc), start=(kc == 0), stop=False),
                   waits=(wwaits + [("dq", xload), ("dve", val("dve"))])
                   if kc == 0 else ())
            em("tensor", lambda t, g=g, pp=pp, M=M: t.matmul(
                pp[0:M, :], csrow[0:1, g * 128:g * 128 + M], mrow,
                start=False, stop=True),
               waits=[("dve", row_done)], tok="pe")
            mmdone = val("pe")
            if g < 48:
                state["wfree"][g % 2] = ("pe", mmdone)
            dest_fn(pp, mmdone)

        def d_beta(pp, mmdone):
            em("vector", lambda v, pp=pp: v.tensor_mul(
                out=bpre[:], in0=rbc[0:16, :], in1=pp[0:16, :]),
               waits=[("pe", mmdone), ("dve", rbc_done)], tok="dve")
            em("scalar", lambda s: s.activation(out=beta_b[:], in_=bpre[:],
                                                func=AF.Sigmoid),
               waits=[("dve", val("dve"))], tok="act")
        def d_f(pp, mmdone):
            em("vector", lambda v, pp=pp: v.tensor_mul(
                out=fga[:, 0:TILE], in0=rbc[:], in1=pp[:]),
               waits=[("pe", mmdone), ("dve", rbc_done)], tok="dve")
        def d_ga(pp, mmdone):
            em("vector", lambda v, pp=pp: v.tensor_mul(
                out=fga[:, TILE:2 * TILE], in0=rbc[:], in1=pp[:]),
               waits=[("pe", mmdone), ("dve", rbc_done)], tok="dve")
        proj_group(48, d_beta)
        proj_group(49, d_f)
        proj_group(50, d_ga)
        beta_act = val("act")

        for c in range(NCH):
            em("tensor", lambda t, c=c: t.transpose(
                ptr[0:128, 0:16], beta_b[:, c * 128:(c + 1) * 128],
                sid["eye"][0:16, 0:16]),
               waits=[("act", beta_act), ("dve", val("dve"))], tok="pe")
            em("vector", lambda v, c=c: v.tensor_copy(
                out=betaT[:, c * 16:(c + 1) * 16], in_=ptr[0:128, 0:16]),
               waits=[("pe", val("pe"))], tok="dve")

        for hh2 in range(CFG["nhalf"]):
            def d_q(h2):
                def f(pp, mmdone):
                    em("vector", lambda v, h2=h2, pp=pp: v.tensor_copy(
                        out=qbar[:, h2 * TILE:(h2 + 1) * TILE], in_=pp[:]),
                       waits=[("pe", mmdone)], tok="dve")
                return f
            def d_k(h2):
                def f(pp, mmdone):
                    em("vector", lambda v, h2=h2, pp=pp: v.tensor_copy(
                        out=kbar[:, h2 * TILE:(h2 + 1) * TILE], in_=pp[:]),
                       waits=[("pe", mmdone)], tok="dve")
                return f
            def d_v(h2):
                def f(pp, mmdone):
                    em("vector", lambda v, h2=h2, pp=pp: v.tensor_mul(
                        out=vcv[:, h2 * CV + 3:h2 * CV + 3 + TILE],
                        in0=rbc[:], in1=pp[:]),
                       waits=[("pe", mmdone), ("dve", rbc_done)], tok="dve")
                return f
            for h2 in range(NH):
                proj_group(hh2 * NH + h2, d_q(h2))
            for h2 in range(NH):
                proj_group(16 + hh2 * NH + h2, d_k(h2))
            for h2 in range(NH):
                proj_group(32 + hh2 * NH + h2, d_v(h2))
            proj_dve = val("dve")
            proj_act = val("act")
            if ti == 0 and hh2 == 0:
                dump("dbg_qpre", qbar[:])
                dump("dbg_kpre", kbar[:])

            for i3, srcT in enumerate((qcv, kcv, vcv)):
                for h2 in range(NH):
                    toff = (hh2 * 3 + i3) * NH * 3 + h2 * 3
                    em("vector", lambda v, srcT=srcT, h2=h2, toff=toff: v.tensor_copy(
                        out=srcT[:, h2 * CV:h2 * CV + 3],
                        in_=tails[:, toff:toff + 3]),
                       waits=([("dve", proj_dve)] if (i3 == 0 and h2 == 0) else ()),
                       tok="dve")
            halo_in = val("dve")

            em("scalar", lambda s: s.activation(out=x2t[:, 0:NH * TILE], in_=qbar[:],
                                                func=AF.Square),
               waits=[("dve", proj_dve)], tok="act")
            em("scalar", lambda s: s.activation(out=x2t[:, NH * TILE:16 * TILE],
                                                in_=kbar[:], func=AF.Square),
               tok="act")
            sq2 = val("act")
            for g2 in range(16):
                em("tensor", lambda t, g2=g2: t.matmul(
                    pstat[:, :], sid["oh16"][:, g2 * 16:g2 * 16 + 16],
                    x2t[:, g2 * TILE:(g2 + 1) * TILE],
                    start=(g2 == 0), stop=(g2 == 15)),
                   waits=([("act", sq2)] if g2 == 0 else ()),
                   tok=("pe" if g2 == 15 else None))
            l2mm = val("pe")
            em("scalar", lambda s: s.activation(out=l2pk[:], in_=pstat[:], func=AF.Ln,
                                                bias=fc1(1, 16)),
               waits=[("pe", l2mm)], tok="act")
            em("scalar", lambda s: s.activation(out=l2pk[:], in_=l2pk[:], func=AF.Exp,
                                                scale=-0.5), tok="act")
            l2r = val("act")
            em("vector", lambda v: v.tensor_copy(out=l2st[:], in_=l2pk[:]),
               waits=[("act", l2r)], tok="dve")
            em("sync", lambda s, ti=ti: s.dma_start(
                out=d["rowl"][ti].rearrange("(p f) -> p f", p=16), in_=l2st[:]),
               waits=[("dve", val("dve"))], tok="ds", dma=True)
            em("sync", lambda s, ti=ti: s.dma_start(
                out=l2row[0:1, :], in_=d["rowl"][ti].rearrange("(o f) -> o f", o=1)),
               waits=[("ds", val("ds"))], tok="ds", dma=True)
            l2pack = val("ds")
            for g2 in range(16):
                em("tensor", lambda t, g2=g2: t.matmul(
                    pbc[:], sid["ones"][0:1, :], l2row[0:1, g2 * TILE:(g2 + 1) * TILE],
                    start=True, stop=True),
                   waits=[("ds", l2pack), ("dve", val("dve"))], tok="pe")
                bc_mm = val("pe")
                if g2 < 8:
                    em("vector", lambda v, g2=g2: v.tensor_mul(
                        out=qcv[:, g2 * CV + 3:g2 * CV + 3 + TILE],
                        in0=qbar[:, g2 * TILE:(g2 + 1) * TILE], in1=pbc[:]),
                       waits=[("pe", bc_mm), ("dve", halo_in)], tok="dve")
                else:
                    h2 = g2 - 8
                    em("vector", lambda v, h2=h2: v.tensor_mul(
                        out=kcv[:, h2 * CV + 3:h2 * CV + 3 + TILE],
                        in0=kbar[:, h2 * TILE:(h2 + 1) * TILE], in1=pbc[:]),
                       waits=[("pe", bc_mm)], tok="dve")
            l2_done = val("dve")
            if ti == 0 and hh2 == 0:
                dump("dbg_qcv", qcv[:])
                dump("dbg_kcv", kcv[:])
                dump("dbg_vcv", vcv[:])

            sc_par = [x2t[:, 0:TILE], x2t[:, TILE:2 * TILE]]
            silu_tok = [proj_act, proj_act]
            for i3, (srcT, dstT, coff) in enumerate(
                    [(qcv, qbar, 64), (kcv, kbar, 128), (vcv, vbar, 192)]):
                for h2 in range(NH):
                    par = (i3 * NH + h2) % 2
                    sc = sc_par[par]
                    co = coff + (hh2 * NH + h2) * 4
                    s0 = h2 * CV
                    sv = lambda off, srcT=srcT, s0=s0: srcT[:, s0 + off:s0 + off + TILE]
                    em("vector", lambda v, sv=sv, co=co, sc=sc: v.tensor_scalar(
                        out=sc, in0=sv(3), scalar1=smallf[:, co + 3:co + 4],
                        scalar2=None, op0=ALU.mult),
                       waits=[("dve", l2_done), ("act", silu_tok[par])], tok="dve")
                    for gamma in (2, 1, 0):
                        em("vector", lambda v, sv=sv, co=co, gamma=gamma, sc=sc:
                           v.scalar_tensor_tensor(
                               out=sc, in0=sv(gamma),
                               scalar=smallf[:, co + gamma:co + gamma + 1],
                               in1=sc, op0=ALU.mult, op1=ALU.add), tok="dve")
                    em("scalar", lambda s, dstT=dstT, h2=h2, sc=sc: s.activation(
                        out=dstT[:, h2 * TILE:(h2 + 1) * TILE], in_=sc,
                        func=AF.Silu),
                       waits=[("dve", val("dve"))], tok="act")
                    silu_tok[par] = val("act")
            conv_act = val("act")
            for i3, srcT in enumerate((qcv, kcv, vcv)):
                for h2 in range(NH):
                    toff = (hh2 * 3 + i3) * NH * 3 + h2 * 3
                    em("vector", lambda v, srcT=srcT, h2=h2, toff=toff: v.tensor_copy(
                        out=tails[:, toff:toff + 3],
                        in_=srcT[:, h2 * CV + TILE:h2 * CV + TILE + 3]),
                       waits=([("act", conv_act)] if (i3 == 0 and h2 == 0) else ()),
                       tok="dve")

            if ti == 0 and hh2 == 0:
                dump("dbg_qbar", qbar[:])
                dump("dbg_kbar", kbar[:])
                dump("dbg_vbar", vbar[:])
                dump("dbg_beta", beta_b[:])
            for h2 in range(CFG["nheads"]):
                _recur_head(ti, hh2, hh2 * NH + h2)

    for ti in range(CFG["nt"]):
        emit_tile(ti)
    mix_all = val("dq")

    mix_ctx.close()

    p2 = ExitStack()
    E2 = p2.enter_context
    sb2 = lambda n, shp, dt=BF16: E2(nc.sbuf_tensor(n, shp, dt))
    mixs = sb2("mixs", [128, 16 * FT])
    xf = sb2("xf", [128, 16 * FT], F32)

    yt = sb2("yt", [128, 16 * FT])
    acts = sb2("acts", [128, 64 * FT])
    x1b = acts[:, 48 * FT:64 * FT]
    wfbuf = [sb2(f"wf{i}", [128, 16 * 128]) for i in range(2)]
    wdb = sb2("wdb", [128, IFF])
    oub = [sb2("oub0", [128, FT], F32)]
    oub.append(oub[0])
    m2bc = mixs[:, FT:2 * FT]
    r2bc = mixs[:, 2 * FT:3 * FT]
    srow = mixs[0:1, 3 * FT:5 * FT]

    def dump2(name, src_ap):
        if name not in dbg:
            return
        em("sync", lambda s, name=name, src_ap=src_ap: s.dma_start(
            out=d[name][:], in_=src_ap),
           waits=[("dve", val("dve")), ("act", val("act")), ("pe", val("pe"))],
           tok="ds", dma=True)

    gsc = lambda ic: ffsc[:, ic:ic + 1]
    usc = lambda ic: ffsc[:, 64 + ic:64 + ic + 1]
    dsc = lambda hc: ffsc[:, 128 + hc:128 + hc + 1]

    ffn_first = [True]
    oub_free = [("ds", val("ds"))]

    def emit_ft(fi):
        t0 = fi * FT
        mld = em("sync", lambda s: s.dma_start(
            out=mixs[:].rearrange("p (kc t) -> p kc t", kc=16),
            in_=d["mixs_d"].rearrange("(kc kp) t -> kp kc t", kp=128)[:, :, t0:t0 + FT]),
            waits=[("dq", mix_all), ("dve", val("dve")), ("act", val("act")),
                   ("pe", val("pe"))], tok="ds", dma=True)
        xld = em("gpsimd", lambda g: g.dma_start(
            out=xf[:].rearrange("p (kc t) -> p kc t", kc=16),
            in_=d["x2"].rearrange("(kc kp) t -> kp kc t", kp=128)[:, :, t0:t0 + FT]),
            waits=[("dq", mix_all)], tok="dq", dma=True)

        wo_done = [0]
        for oc in range(16):
            bi = oc % 2
            wl = em("gpsimd", lambda g, oc=oc, bi=bi: g.dma_start(
                out=wfbuf[bi][:], in_=d["wo"][oc]),
                waits=[("pe", wo_done[0])], tok="dq", dma=True)
            pp = pproj[oc % 2]
            for kc in range(16):
                em("tensor", lambda t, kc=kc, bi=bi, pp=pp: t.matmul(
                    pp[:], wfbuf[bi][:, wsl(kc)],
                    mixs[:, kc * FT:(kc + 1) * FT],
                    start=(kc == 0), stop=(kc == 15)),
                   waits=([("dq", wl), ("ds", mld), ("dve", val("dve"))]
                          if kc == 0 else ()),
                   tok=("pe" if kc == 15 else None))
            wo_done[0] = val("pe")
            em("vector", lambda v, oc=oc, pp=pp: v.tensor_add(
                out=xf[:, oc * FT:(oc + 1) * FT], in0=xf[:, oc * FT:(oc + 1) * FT],
                in1=pp[:]),
               waits=[("pe", wo_done[0]), ("dq", xld)], tok="dve")
            em("vector", lambda v, oc=oc: v.tensor_copy(
                out=x1b[:, oc * FT:(oc + 1) * FT], in_=xf[:, oc * FT:(oc + 1) * FT]),
               tok="dve")
        x1_done = val("dve")
        if fi == 0:
            dump2("dbg_x1", xf[:])

        for kc in range(16):
            em("tensor", lambda t, kc=kc: t.matmul(
                pstat[0:1, :], sid["sum2048"], x1b[:, kc * FT:(kc + 1) * FT],
                start=(kc == 0), stop=(kc == 15)),
               waits=([("dve", x1_done)] if kc == 0 else ()),
               tok=("pe" if kc == 15 else None))
        em("vector", lambda v: v.tensor_copy(out=rowst[0:1, 0:FT], in_=pstat[0:1, :]),
           waits=[("pe", val("pe"))], tok="dve")
        m2ev = val("dve")
        em("scalar", lambda s: s.activation(out=acts[:, 0:16 * FT], in_=x1b[:],
                                            func=AF.Square),
           waits=[("dve", x1_done)], tok="act")
        sqd = val("act")
        for kc in range(16):
            em("tensor", lambda t, kc=kc: t.matmul(
                pstat[0:1, :], sid["sum2048"], acts[:, kc * FT:(kc + 1) * FT],
                start=(kc == 0), stop=(kc == 15)),
               waits=([("act", sqd), ("dve", m2ev)] if kc == 0 else ()),
               tok=("pe" if kc == 15 else None))
        em("vector", lambda v: v.tensor_copy(out=rowst[0:1, FT:2 * FT],
                                             in_=pstat[0:1, :]),
           waits=[("pe", val("pe"))], tok="dve")
        em("vector", lambda v: v.tensor_mul(out=rowst[0:1, 2 * FT:3 * FT],
                                            in0=rowst[0:1, 0:FT],
                                            in1=rowst[0:1, 0:FT]), tok="dve")
        em("vector", lambda v: v.tensor_sub(out=rowst[0:1, 2 * FT:3 * FT],
                                            in0=rowst[0:1, FT:2 * FT],
                                            in1=rowst[0:1, 2 * FT:3 * FT]), tok="dve")
        vd2 = val("dve")
        em("scalar", lambda s: s.activation(out=rowst[0:1, 3 * FT:4 * FT],
                                            in_=rowst[0:1, 2 * FT:3 * FT],
                                            func=AF.Ln, bias=smallf[0:1, 32:33]),
           waits=[("dve", vd2)], tok="act")
        em("scalar", lambda s: s.activation(out=rowst[0:1, 4 * FT:5 * FT],
                                            in_=rowst[0:1, 3 * FT:4 * FT],
                                            func=AF.Exp, scale=-0.5), tok="act")
        r2a = val("act")
        em("vector", lambda v: v.tensor_copy(out=rowbf[0:1, 0:FT],
                                             in_=rowst[0:1, 0:FT]),
           waits=[("act", r2a)], tok="dve")
        em("vector", lambda v: v.tensor_copy(out=rowbf[0:1, FT:2 * FT],
                                             in_=rowst[0:1, 4 * FT:5 * FT]), tok="dve")
        rw2 = val("dve")
        em("tensor", lambda t: t.matmul(pbc[:], sid["ones"][0:1, :],
                                        rowbf[0:1, 0:FT], start=True, stop=True),
           waits=[("dve", rw2)], tok="pe")
        em("vector", lambda v: v.tensor_copy(out=m2bc[:], in_=pbc[:]),
           waits=[("pe", val("pe"))], tok="dve")
        em("tensor", lambda t: t.matmul(pbc[:], sid["ones"][0:1, :],
                                        rowbf[0:1, FT:2 * FT], start=True, stop=True),
           waits=[("dve", val("dve"))], tok="pe")
        em("vector", lambda v: v.tensor_copy(out=r2bc[:], in_=pbc[:]),
           waits=[("pe", val("pe"))], tok="dve")
        bc2 = val("dve")
        for kc in range(16):
            em("vector", lambda v, kc=kc: v.tensor_sub(
                out=yt[:, kc * FT:(kc + 1) * FT], in0=xf[:, kc * FT:(kc + 1) * FT],
                in1=m2bc[:]),
               waits=([("dve", bc2)] if kc == 0 else ()), tok="dve")
            em("vector", lambda v, kc=kc: v.tensor_mul(
                out=yt[:, kc * FT:(kc + 1) * FT], in0=yt[:, kc * FT:(kc + 1) * FT],
                in1=r2bc[:]), tok="dve")
        y_done = val("dve")
        if fi == 0:
            dump2("dbg_y", yt[:])
            dump2("dbg_rows2", rowst[0:1, 0:5 * FT])
            dump2("dbg_r2bc", r2bc[:])

        for kc in range(16):
            em("tensor", lambda t, kc=kc: t.matmul(
                pstat[0:1, :], sid["sum1"], yt[:, kc * FT:(kc + 1) * FT],
                start=(kc == 0), stop=(kc == 15)),
               waits=([("dve", y_done)] if kc == 0 else ()),
               tok=("pe" if kc == 15 else None))
        em("vector", lambda v: v.tensor_scalar(out=srow[0:1, 0:FT],
                                               in0=pstat[0:1, :], scalar1=-128.0,
                                               scalar2=None, op0=ALU.mult),
           waits=[("pe", val("pe"))], tok="dve")
        sy_done = val("dve")

        act_done = [0]
        for ic in range(64):
            bi = ic % 2
            if ffn_first[0]:
                wlg = em("gpsimd", lambda g, ic=ic, bi=bi: g.dma_start(
                    out=wfbuf[bi][:], in_=d["wg"][ic]),
                    waits=[("pe", val("pe"))], tok="dq", dma=True)
            else:
                wlg = em("gpsimd", lambda g, ic=ic, bi=bi: g.dma_start(
                    out=wfbuf[bi][:], in_=d["wg_bf"][ic]),
                    waits=[("pe", val("pe"))], tok="dq", dma=True)
            pp = pproj[0]
            for kc in range(16):
                em("tensor", lambda t, kc=kc, bi=bi, pp=pp: t.matmul(
                    pp[:], wfbuf[bi][:, wsl(kc)], yt[:, kc * FT:(kc + 1) * FT],
                    start=(kc == 0), stop=False),
                   waits=([("dq", wlg), ("dve", sy_done), ("act", act_done[0])]
                          if kc == 0 else ()))
            em("tensor", lambda t, pp=pp: t.matmul(
                pp[:], sid["ones"][0:1, :], srow[0:1, 0:FT],
                start=False, stop=True), tok="pe")
            g_mm = val("pe")
            if ffn_first[0]:
                em("sync", lambda s, ic=ic, bi=bi: s.dma_start(
                    out=d["wg_bf"][ic], in_=wfbuf[bi][:]),
                   waits=[("dq", wlg)], tok="ds", dma=True)
            em("scalar", lambda s, ic=ic, pp=pp: s.activation(
                out=mixs[:, 0:FT], in_=pp[:], func=AF.Silu, scale=gsc(ic)),
               waits=[("pe", g_mm)], tok="act")
            silu_g = val("act")
            if ffn_first[0]:
                wlu = em("gpsimd", lambda g, ic=ic, bi=bi: g.dma_start(
                    out=wfbuf[bi][:], in_=d["wu"][ic]),
                    waits=[("pe", g_mm)], tok="dq", dma=True)
            else:
                wlu = em("gpsimd", lambda g, ic=ic, bi=bi: g.dma_start(
                    out=wfbuf[bi][:], in_=d["wu_bf"][ic]),
                    waits=[("pe", g_mm)], tok="dq", dma=True)
            for kc in range(16):
                em("tensor", lambda t, kc=kc, bi=bi, pp=pp: t.matmul(
                    ppc[0][:, 0:FT], wfbuf[bi][:, wsl(kc)],
                    yt[:, kc * FT:(kc + 1) * FT],
                    start=(kc == 0), stop=False),
                   waits=([("dq", wlu), ("act", silu_g), ("dve", val("dve"))]
                          if kc == 0 else ()))
            em("tensor", lambda t: t.matmul(
                ppc[0][:, 0:FT], sid["ones"][0:1, :], srow[0:1, 0:FT],
                start=False, stop=True), tok="pe")
            u_mm = val("pe")
            if ffn_first[0]:
                em("sync", lambda s, ic=ic, bi=bi: s.dma_start(
                    out=d["wu_bf"][ic], in_=wfbuf[bi][:]),
                   waits=[("dq", wlu)], tok="ds", dma=True)
            em("vector", lambda v, ic=ic: v.scalar_tensor_tensor(
                out=acts[:, ic * FT:(ic + 1) * FT], in0=ppc[0][:, 0:FT],
                scalar=usc(ic), in1=mixs[:, 0:FT],
                op0=ALU.mult, op1=ALU.mult),
               waits=[("pe", u_mm)], tok="dve")
            act_done[0] = val("act")
        ffn_acts = val("dve")

        for ic in range(64):
            em("tensor", lambda t, ic=ic: t.matmul(
                pstat[0:1, :], sid["sum1"], acts[:, ic * FT:(ic + 1) * FT],
                start=(ic == 0), stop=(ic == 63)),
               waits=([("dve", ffn_acts)] if ic == 0 else ()),
               tok=("pe" if ic == 63 else None))
        em("vector", lambda v: v.tensor_scalar(out=srow[0:1, FT:2 * FT],
                                               in0=pstat[0:1, :], scalar1=-128.0,
                                               scalar2=None, op0=ALU.mult),
           waits=[("pe", val("pe"))], tok="dve")
        sa_done = val("dve")

        for hc in range(16):
            if ffn_first[0]:
                wld = em("gpsimd", lambda g, hc=hc: g.dma_start(
                    out=wdb[:], in_=d["wd"][hc]),
                    waits=[("pe", val("pe"))], tok="dq", dma=True)
            else:
                wld = em("gpsimd", lambda g, hc=hc: g.dma_start(
                    out=wdb[:], in_=d["wd_bf"][hc]),
                    waits=[("pe", val("pe"))], tok="dq", dma=True)
            pp = pproj[hc % 2]
            for ic in range(64):
                em("tensor", lambda t, ic=ic, pp=pp: t.matmul(
                    pp[:], wdb[:, ic * 128:(ic + 1) * 128],
                    acts[:, ic * FT:(ic + 1) * FT],
                    start=(ic == 0), stop=False),
                   waits=([("dq", wld), ("dve", val("dve"))] if ic == 0 else ()))
            em("tensor", lambda t, pp=pp: t.matmul(
                pp[:], sid["ones"][0:1, :], srow[0:1, FT:2 * FT],
                start=False, stop=True), tok="pe")
            d_mm = val("pe")
            if ffn_first[0]:
                em("sync", lambda s, hc=hc: s.dma_start(
                    out=d["wd_bf"][hc], in_=wdb[:]),
                   waits=[("dq", wld)], tok="ds", dma=True)
            ob = oub[0]
            em("vector", lambda v, hc=hc, pp=pp, ob=ob: v.scalar_tensor_tensor(
                out=ob[:], in0=pp[:], scalar=dsc(hc),
                in1=xf[:, hc * FT:(hc + 1) * FT], op0=ALU.mult, op1=ALU.add),
               waits=[("pe", d_mm), oub_free[0]], tok="dve")
            st = em("sync", lambda s, hc=hc, fi=fi, ob=ob: s.dma_start(
                out=d["out"][hc * 128:(hc + 1) * 128, fi * FT:(fi + 1) * FT],
                in_=ob[:]),
                waits=[("dve", val("dve"))], tok="do", dma=True)
            oub_free[0] = ("do", st)
        ffn_first[0] = False

    if CFG["p2"]:
        for fi in range(CFG["nft"]):
            emit_ft(fi)
        em("sync", lambda s: s.nop(), waits=[("do", val("do"))])
    else:
        em("sync", lambda s: s.nop(),
           waits=[("dq", val("dq")), ("ds", val("ds")), ("dve", val("dve")),
                  ("act", val("act")), ("pe", val("pe"))])
    p2.close()





def kernel(**inputs):
    (x_fm, wqkv, wsmall, wo_pack, wg_pack, wu_pack, wd_pack, ffnsc,
     mask_pack) = _host_prep(inputs)
    nc = build_program()
    in_maps = []
    for c in range(8):
        b, hh = c >> 1, c & 1
        mselv = np.zeros((128, NT), np.float32)
        mselv[:, hh * 4:(hh + 1) * 4] = 1.0
        in_maps.append({
            "x": x_fm[b],
            "x2": np.ascontiguousarray(x_fm[b][:, hh * TOK2:(hh + 1) * TOK2]),
            "wqkv": wqkv,
            "wsmall": wsmall,
            "maskp": mask_pack,
            "wo": wo_pack,
            "wg": wg_pack,
            "wu": wu_pack,
            "wd": wd_pack,
            "ffnsc": ffnsc,
            "msel": mselv,
        })
    res = run_bass_kernel_spmd(nc, in_maps, core_ids=list(range(8)))
    out = np.empty((B, S, HID), np.float32)
    for c in range(8):
        b, hh = c >> 1, c & 1
        out[b, hh * TOK2:(hh + 1) * TOK2, :] = res.results[c]["out"].T
    return out
